# version 01 (final)
# baseline (speedup 1.0000x reference)
import numpy as np
from contextlib import ExitStack

import concourse.bass as bass
import concourse.mybir as mybir
from concourse.bass_utils import run_bass_kernel_spmd

F32 = mybir.dt.float32
AF = mybir.ActivationFunctionType
ALU = mybir.AluOpType
AX = mybir.AxisListType

D = 1024
DC = 8
NB = 2
S = 2048
T = NB * S
SL = 512
NSL = T // SL
NEG = -1.0e30
MAGIC = 12582912.0
QMAX = 32767.0

_TAPS = ()
_PHASES = ()

C_AMAX, C_AMAXT = 0, 1
C_SX, C_ISX = 2, 3
C_AMG, C_AMU = 4, 5
C_SG, C_ISG = 6, 7
C_SU, C_ISU = 8, 9
C_NEGMAX, C_RSUM, C_RCP = 10, 11, 12
C_CMUL, C_SG2 = 13, 14
C_ZERO8 = 16
C_AMG2, C_AMO = 24, 25
C_SO, C_ISO = 26, 27
C_TMP, C_TMP2 = 28, 29
C_EPS = 30
C_BLK = 32
C_GB1 = 224
C_B2C = 232
C_GB2 = 240
C_HB = 244
C_GBC = 260
C_G = 276
C_B1C = 280
C_MB1 = {0: 128, 1: 160}
C_MB2 = {0: 192, 1: 200}


class Ctr:
    __slots__ = ("h", "v")

    def __init__(self, h):
        self.h = h
        self.v = 0

    def inc(self, instr, k=1):
        instr.then_inc(self.h, k)
        self.v += k
        return self.v


class DmaQ:

    def __init__(self, sems):
        self.sems = sems
        self.tot = [0] * len(sems)
        self.cur = -1
        self.plan = []
        self.pi = 0
        self.open = None

    def plan_granule(self, ndma):
        self.cur = (self.cur + 1) % len(self.sems)
        i = self.cur
        gate = self.tot[i]
        self.tot[i] += 16 * ndma
        self.plan.append((i, gate, ndma))
        return (self.sems[i], self.tot[i])

    def begin(self, eng):
        i, gate, ndma = self.plan[self.pi]
        self.pi += 1
        if gate > 0:
            eng.wait_ge(self.sems[i], gate)
        self.open = [i, ndma]

    def dma(self, eng, out, in_):
        i, left = self.open
        assert left > 0
        self.open[1] -= 1
        eng.dma_start(out=out, in_=in_).then_inc(self.sems[i], 16)

    def end(self):
        assert self.open[1] == 0
        self.open = None

    def drain(self, eng):
        for i, t in enumerate(self.tot):
            if t > 0:
                eng.wait_ge(self.sems[i], t)


def vfence(vector, b):
    vector.wait_ge(b.vE.h, b.vE.v)


class B:
    pass


def l0_tiles():
    out = []
    for ti in range(16):
        if ti == 0:
            out.append((ti, [0], "m0"))
        elif ti == 1:
            out.append((ti, [0, 1], "m1"))
        else:
            out.append((ti, [ti - 2, ti - 1, ti], "mI"))
    return out


def l1_tiles():
    out = []
    for ti in range(16):
        if ti == 0:
            out.append((ti, [0, 1], "b0"))
        elif ti == 15:
            out.append((ti, [14, 15], "bL"))
        else:
            out.append((ti, [ti - 1, ti, ti + 1], "bI"))
    return out


def _mask_consts():
    r = np.arange(128)[:, None]
    c1 = np.arange(128)[None, :]
    c2 = np.arange(256)[None, :]
    c3 = np.arange(384)[None, :]
    m = {}
    m["m0"] = np.where(c1 <= r, 0.0, NEG)
    m["m1"] = np.where(c2 <= r + 128, 0.0, NEG)
    m["mI"] = np.where((c3 > r) & (c3 <= r + 256), 0.0, NEG)
    m["b0"] = np.where(np.abs(c2 - r) <= 128, 0.0, NEG)
    m["bI"] = np.where((c3 >= r) & (c3 <= r + 256), 0.0, NEG)
    m["bL"] = np.where(c2 >= r, 0.0, NEG)
    return {k: v.astype(np.float32) for k, v in m.items()}


def build_nc():
    nc = bass.Bass("TRN2", target_bir_lowering=False, detect_race_conditions=False)
    b = B()
    b.nc = nc

    def din(name, shape):
        return nc.dram_tensor(name, shape, F32, kind="ExternalInput")

    b.x0 = din("x0", [D, T])
    b.WqT = din("WqT", [D, D])
    b.WkT = din("WkT", [D, D])
    b.WvT = din("WvT", [D, D])
    b.WoT = din("WoT", [D, D])
    b.w_gate = din("w_gate", [3072, D])
    b.w_up = din("w_up", [3072, D])
    b.w_down = din("w_down", [D, 3072])
    b.wqkvT = din("wqkvT", [D, 3072])
    b.woutT = din("woutT", [D, D])
    b.gamma = din("gamma", [128, DC])
    b.beta = din("beta", [128, DC])
    b.gw1T = din("gw1T", [D, D])
    b.gw2T = din("gw2T", [D, 8])
    b.gb1d = din("gb1d", [128, DC])
    b.gb2d = din("gb2d", [8, 1])
    b.W1T = din("W1T", [D, 4096])
    b.b1d = din("b1d", [128, 32])
    b.W2T = din("W2T", [4096, D])
    b.b2d = din("b2d", [128, DC])
    b.TW1T = din("TW1T", [8, D, 4096])
    b.Tb1 = din("Tb1", [8, 4096])
    b.TW2T = din("TW2T", [8, 4096, D])
    b.Tb2 = din("Tb2", [8, D])
    b.ident_d = din("ident_d", [128, 128])
    b.ident2_d = din("ident2_d", [128, 64])
    b.ones_d = din("ones_d", [128, 128])
    b.masks_d = {k: din("mask_" + k, list(v.shape)) for k, v in _mask_consts().items()}

    b.y = nc.dram_tensor("y", [D, T], F32, kind="ExternalOutput")

    def dint(name, shape):
        return nc.dram_tensor(name, shape, F32)

    b.qkvb = dint("qkvb", [3072, T])
    b.outTb = dint("outTb", [D, T])
    b.x1b = dint("x1b", [D, T])
    b.x2b = dint("x2b", [D, T])
    b.x3b = dint("x3b", [D, T])
    b.upb = dint("upb", [3072, T])
    b.gateb = dint("gateb", [3072, T])
    b.g2vb = dint("g2vb", [3072, T])
    b.mW1Tb = {0: dint("mW1Tb0", [D, 4096]), 1: dint("mW1Tb1", [D, 4096])}
    b.mW2Tb = {0: dint("mW2Tb0", [4096, D]), 1: dint("mW2Tb1", [4096, D])}
    b.gdram = dint("gdram", [8, 2])
    b.ccin = dint("ccin", [1, 8])
    b.ccout = [dint(f"ccout{i}", [1, 8]) for i in range(4)]

    b.taps = {}
    for t in _TAPS:
        src = getattr(b, t)
        b.taps[t] = nc.dram_tensor("tap_" + t, list(src.shape), F32,
                                   kind="ExternalOutput")

    with ExitStack() as ctx:
        b.xn = ctx.enter_context(nc.sbuf_tensor("xn", [128, DC * T], F32))
        b.stg = ctx.enter_context(nc.sbuf_tensor("stg", [128, 8192], F32))
        b.wst = ctx.enter_context(nc.sbuf_tensor("wst", [128, 4096], F32))
        b.ident = ctx.enter_context(nc.sbuf_tensor("ident", [128, 128], F32))
        b.ident2 = ctx.enter_context(nc.sbuf_tensor("ident2", [128, 64], F32))
        b.ones = ctx.enter_context(nc.sbuf_tensor("ones", [128, 128], F32))
        b.masks = {k: ctx.enter_context(nc.sbuf_tensor("msk_" + k, list(v.shape), F32))
                   for k, v in _mask_consts().items()}
        b.gb = ctx.enter_context(nc.sbuf_tensor("gb", [128, 2 * DC], F32))
        b.sc = ctx.enter_context(nc.sbuf_tensor("sc", [128, 320], F32))
        b.psum = [ctx.enter_context(nc.psum_tensor(f"pb{i}", [128, 512], F32))
                  for i in range(8)]

        b.qW = DmaQ([ctx.enter_context(nc.semaphore(f"qW{i}")) for i in range(4)])
        b.qX = DmaQ([ctx.enter_context(nc.semaphore(f"qX{i}")) for i in range(4)])
        b.qS = DmaQ([ctx.enter_context(nc.semaphore(f"qS{i}")) for i in range(3)])
        b.tM = Ctr(ctx.enter_context(nc.semaphore("tM")))
        b.vE = Ctr(ctx.enter_context(nc.semaphore("vE")))
        b.sE = Ctr(ctx.enter_context(nc.semaphore("sE")))
        b.cc = Ctr(ctx.enter_context(nc.semaphore("cc")))

        _build_phases(b)

    return nc


def _build_phases(b):
    phases = _PHASES or (
        "init",
        "ln1", "qkv0", "attn0", "oproj0",
        "ln2", "gmax0", "quant0", "gup", "gmax1", "sigch", "gmax2",
        "omul", "gmax3", "wdqdown", "down",
        "ln3", "qkv1", "attn1", "oproj1",
        "ln4", "gate1", "gvec", "wmod", "ffnmain",
        "taps",
    )
    for p in phases:
        globals()["_ph_" + p](b)


def _ph_init(b):
    nc = b.nc
    b.qW.plan_granule(15)
    with nc.Block("init") as blk:
        @blk.sync
        def _(sync):
            b.qW.begin(sync)
            b.qW.dma(sync, b.ident[:, :], b.ident_d.ap())
            b.qW.dma(sync, b.ident2[:, :], b.ident2_d.ap())
            b.qW.dma(sync, b.ones[:, :], b.ones_d.ap())
            for k, h in b.masks.items():
                b.qW.dma(sync, h[:, :], b.masks_d[k].ap())
            b.qW.dma(sync, b.gb[:, 0:DC], b.gamma.ap())
            b.qW.dma(sync, b.gb[:, DC:2 * DC], b.beta.ap())
            b.qW.dma(sync, b.sc[:, C_GB1:C_GB1 + 8], b.gb1d.ap())
            b.qW.dma(sync, b.sc[:, C_B1C:C_B1C + 32], b.b1d.ap())
            b.qW.dma(sync, b.sc[:, C_B2C:C_B2C + 8], b.b2d.ap())
            b.qW.dma(sync, b.sc[0:8, C_GB2:C_GB2 + 1], b.gb2d.ap())
            b.qW.end()
            b.qW.drain(sync)

        @blk.vector
        def _(vector):
            vector.memset(b.sc[:, C_ZERO8:C_ZERO8 + 8], 0.0)
            vector.memset(b.sc[:, C_EPS:C_EPS + 1], 1e-5)


def _emit_ln(b, name, src_dram, affine=False, amax=False, spill=None):
    nc = b.nc
    SQ = 4096
    STT = 6144

    def xslot(sl, dc):
        buf = b.stg if sl % 2 == 0 else b.wst
        return buf[:, dc * 512:dc * 512 + 512]

    t0, v0, s0 = b.tM.v, b.vE.v, b.sE.v
    tok_x = [b.qW.plan_granule(DC) for _ in range(NSL)]
    if spill is not None:
        tok_sp = [b.qX.plan_granule(DC) for _ in range(NSL)]
    to, vo, so = {}, {}, {}
    nt, nv, ns = t0, v0, s0
    for sl in range(NSL):
        nt += 1
        to[("pm", sl)] = nt
        for dc in range(DC):
            ns += 1
            so[("sq", sl, dc)] = ns
            nt += 1
            to[("pq", sl, dc)] = nt
        ns += 1
        so[("sd", sl)] = ns
        for ev in ("m", "t", "var", "rstd", "mr"):
            nv += 1
            vo[(ev, sl)] = nv
        for dc in range(DC):
            nv += 1
            vo[("mul", sl, dc)] = nv
            nv += 1
            vo[("sub", sl, dc)] = nv
            if affine:
                nv += 1
                vo[("aff", sl, dc)] = nv
            if amax:
                nv += 1
                vo[("red", sl, dc)] = nv
                if not (sl == 0 and dc == 0):
                    nv += 1
                    vo[("mx", sl, dc)] = nv

    def last_v(sl, dc):
        if amax:
            return vo[("red", sl, dc)] if (sl == 0 and dc == 0) else vo[("mx", sl, dc)]
        if affine:
            return vo[("aff", sl, dc)]
        return vo[("sub", sl, dc)]

    with nc.Block(name) as blk:
        @blk.sync
        def _(sync):
            for sl in range(NSL):
                if sl >= 2:
                    sync.wait_ge(b.vE.h, last_v(sl - 2, DC - 1))
                b.qW.begin(sync)
                for dc in range(DC):
                    b.qW.dma(sync, xslot(sl, dc),
                             src_dram.ap()[dc * 128:(dc + 1) * 128, sl * SL:(sl + 1) * SL])
                b.qW.end()
            b.qW.drain(sync)

        @blk.tensor
        def _(tensor):
            for sl in range(NSL):
                pm = b.psum[2 * (sl % 2)]
                pq = b.psum[2 * (sl % 2) + 1]
                tensor.wait_ge(*tok_x[sl])
                if sl >= 2:
                    tensor.wait_ge(b.vE.h, vo[("var", sl - 2)])
                for dc in range(DC):
                    i = tensor.matmul(pm[:, :], b.ones[:, :], xslot(sl, dc),
                                      start=(dc == 0), stop=(dc == DC - 1))
                b.tM.inc(i)
                for dc in range(DC):
                    tensor.wait_ge(b.sE.h, so[("sq", sl, dc)])
                    i = tensor.matmul(pq[:, :], b.ones[:, :],
                                      b.stg[:, SQ + (dc % 4) * 512:SQ + (dc % 4) * 512 + 512],
                                      start=(dc == 0), stop=(dc == DC - 1))
                    b.tM.inc(i)

        @blk.scalar
        def _(scalar):
            for sl in range(NSL):
                scalar.wait_ge(*tok_x[sl])
                for dc in range(DC):
                    k = sl * DC + dc
                    if k >= 4:
                        scalar.wait_ge(b.tM.h, to[("pq", (k - 4) // DC, (k - 4) % DC)])
                    b.sE.inc(scalar.activation(
                        b.stg[:, SQ + (dc % 4) * 512:SQ + (dc % 4) * 512 + 512],
                        xslot(sl, dc), AF.Square))
                scalar.wait_ge(b.vE.h, vo[("var", sl)])
                b.sE.inc(scalar.activation(
                    b.stg[:, STT + 512:STT + 1024],
                    b.stg[:, STT:STT + 512],
                    AF.Sqrt, bias=b.sc[:, C_EPS:C_EPS + 1], scale=1.0))

        @blk.vector
        def _(vector):
            for sl in range(NSL):
                pm = b.psum[2 * (sl % 2)]
                pq = b.psum[2 * (sl % 2) + 1]
                var = b.stg[:, STT:STT + 512]
                sd = b.stg[:, STT + 512:STT + 1024]
                m = b.stg[:, STT + 1024:STT + 1536]
                t = b.stg[:, STT + 1536:STT + 2048]
                vector.wait_ge(b.tM.h, to[("pq", sl, DC - 1)])
                b.vE.inc(vector.tensor_scalar(m, pm[:, :], 1.0 / D, None, ALU.mult))
                b.vE.inc(vector.tensor_tensor(t, m, m, ALU.mult))
                b.vE.inc(vector.scalar_tensor_tensor(var, pq[:, :], 1.0 / D, t,
                                                     ALU.mult, ALU.subtract))
                vector.wait_ge(b.sE.h, so[("sd", sl)])
                rstd = t
                b.vE.inc(vector.reciprocal(rstd, sd))
                mr = sd
                b.vE.inc(vector.tensor_tensor(mr, m, rstd, ALU.mult))
                for dc in range(DC):
                    xt = xslot(sl, dc)
                    dst = b.xn[:, dc * T + sl * SL:dc * T + sl * SL + SL]
                    b.vE.inc(vector.tensor_tensor(dst, xt, rstd, ALU.mult))
                    b.vE.inc(vector.tensor_tensor(dst, dst, mr, ALU.subtract))
                    if affine:
                        b.vE.inc(vector.tensor_scalar(
                            dst, dst, b.gb[:, dc:dc + 1], b.gb[:, DC + dc:DC + dc + 1],
                            ALU.mult, ALU.add))
                    if amax:
                        first = (sl == 0 and dc == 0)
                        am = b.sc[:, C_AMAX:C_AMAX + 1]
                        amt = b.sc[:, C_AMAXT:C_AMAXT + 1]
                        b.vE.inc(vector.tensor_reduce(
                            am if first else amt, dst, AX.X, ALU.max,
                            apply_absolute_value=True))
                        if not first:
                            vfence(vector, b)
                            b.vE.inc(vector.tensor_tensor(am, am, amt, ALU.max))

        if spill is not None:
            @blk.gpsimd
            def _(gpsimd):
                for sl in range(NSL):
                    gpsimd.wait_ge(b.vE.h, last_v(sl, DC - 1))
                    b.qX.begin(gpsimd)
                    for dc in range(DC):
                        b.qX.dma(gpsimd,
                                 spill.ap()[dc * 128:(dc + 1) * 128, sl * SL:(sl + 1) * SL],
                                 b.xn[:, dc * T + sl * SL:dc * T + sl * SL + SL])
                    b.qX.end()
                b.qX.drain(gpsimd)


def _ph_ln1(b):
    _emit_ln(b, "ln1", b.x0)


def _ph_ln2(b):
    _emit_ln(b, "ln2", b.x1b, amax=True)


def _ph_ln3(b):
    _emit_ln(b, "ln3", b.x2b, affine=True)


def _ph_ln4(b):
    _emit_ln(b, "ln4", b.x3b, spill=b.outTb)


def _emit_proj_from_xn(b, name, w_dram, n_ec, dst_dram, r0=0):
    nc = b.nc
    t0, v0 = b.tM.v, b.vE.v
    tok_w = [b.qW.plan_granule(DC) for _ in range(n_ec)]
    tok_s = [b.qX.plan_granule(1) for _ in range(n_ec * NSL)]
    with nc.Block(name) as blk:
        @blk.sync
        def _(sync):
            for ec in range(n_ec):
                if ec >= 2:
                    sync.wait_ge(b.tM.h, t0 + (ec - 1) * NSL)
                slot = (ec % 2) * 1024
                b.qW.begin(sync)
                for dc in range(DC):
                    b.qW.dma(sync,
                             b.wst[:, slot + dc * 128:slot + dc * 128 + 128],
                             w_dram.ap()[dc * 128:(dc + 1) * 128, ec * 128:(ec + 1) * 128])
                b.qW.end()
            b.qW.drain(sync)

        @blk.tensor
        def _(tensor):
            for ec in range(n_ec):
                slot = (ec % 2) * 1024
                tensor.wait_ge(*tok_w[ec])
                for sl in range(NSL):
                    k = ec * NSL + sl
                    pm = b.psum[k % 4]
                    if k >= 4:
                        tensor.wait_ge(b.vE.h, v0 + k - 3)
                    for dc in range(DC):
                        i = tensor.matmul(
                            pm[:, :],
                            b.wst[:, slot + dc * 128:slot + dc * 128 + 128],
                            b.xn[:, dc * T + sl * SL:dc * T + sl * SL + SL],
                            start=(dc == 0), stop=(dc == DC - 1))
                    b.tM.inc(i)

        @blk.vector
        def _(vector):
            for k in range(n_ec * NSL):
                pm = b.psum[k % 4]
                st = b.stg[:, (k % 8) * 512:(k % 8) * 512 + 512]
                vector.wait_ge(b.tM.h, t0 + k + 1)
                if k >= 8:
                    vector.wait_ge(*tok_s[k - 8])
                b.vE.inc(vector.tensor_copy(st, pm[:, :]))

        @blk.gpsimd
        def _(gpsimd):
            for ec in range(n_ec):
                for sl in range(NSL):
                    k = ec * NSL + sl
                    st = b.stg[:, (k % 8) * 512:(k % 8) * 512 + 512]
                    gpsimd.wait_ge(b.vE.h, v0 + k + 1)
                    b.qX.begin(gpsimd)
                    b.qX.dma(gpsimd,
                             dst_dram.ap()[r0 + ec * 128:r0 + (ec + 1) * 128,
                                           sl * SL:(sl + 1) * SL], st)
                    b.qX.end()
            b.qX.drain(gpsimd)


def _ph_qkv0(b):
    _emit_proj_from_xn(b, "qkv0q", b.WqT, 8, b.qkvb, r0=0)
    _emit_proj_from_xn(b, "qkv0k", b.WkT, 8, b.qkvb, r0=1024)
    _emit_proj_from_xn(b, "qkv0v", b.WvT, 8, b.qkvb, r0=2048)


def _ph_qkv1(b):
    _emit_proj_from_xn(b, "qkv1", b.wqkvT, 24, b.qkvb, r0=0)


def _emit_attn(b, name, tiles):
    nc = b.nc
    QO, KO, VO, OO = 0, 2048, 4096, 6144
    VT, SCB, PR, PT = 0, 2048, 2560, 3072
    PRN = 3584
    NU = NB * 8
    t0, v0, s0 = b.tM.v, b.vE.v, b.sE.v
    tok_in = [b.qW.plan_granule(3) for _ in range(NU)]
    tok_st = [b.qX.plan_granule(1) for _ in range(NU)]

    to, vo, so = {}, {}, {}
    nt, nv, ns = t0, v0, s0
    for u in range(NU):
        for h in range(2):
            for kc in range(16):
                nt += 1
                to[("vt", u, h, kc)] = nt
                ns += 1
                so[("vte", u, h, kc)] = ns
            for ti, chunks, mk in tiles:
                nt += 1
                to[("qk", u, h, ti)] = nt
                nt += 1
                to[("pnt", u, h, ti)] = nt
                nt += 1
                to[("av", u, h, ti)] = nt
                for ev in ("v1", "v2", "v2b", "v3", "v4", "v4b", "v5", "v5b"):
                    nv += 1
                    vo[(ev, u, h, ti)] = nv
                ns += 1
                so[("exp", u, h, ti)] = ns
                ns += 1
                so[("exp2", u, h, ti)] = ns
                ns += 1
                so[("oev", u, h, ti)] = ns

    last_ti = tiles[-1][0]
    vt_seq = [(u, h, kc) for u in range(NU) for h in range(2) for kc in range(16)]
    vt_pos = {k: i for i, k in enumerate(vt_seq)}

    def prev_tile(u, h, ti_idx):
        p = (u * 2 + h) * len(tiles) + ti_idx - 1
        if p < 0:
            return None
        uu, rem = divmod(p, 2 * len(tiles))
        hh, tt = divmod(rem, len(tiles))
        return (uu, hh, tiles[tt][0])

    with nc.Block(name) as blk:
        @blk.sync
        def _(sync):
            for u in range(NU):
                bb, hp = divmod(u, 8)
                if u >= 1:
                    sync.wait_ge(b.tM.h, to[("qk", u - 1, 1, last_ti)])
                b.qW.begin(sync)
                for j, off in ((0, QO), (1024, KO), (2048, VO)):
                    b.qW.dma(sync, b.stg[:, off:off + 2048],
                             b.qkvb.ap()[j + hp * 128:j + hp * 128 + 128,
                                         bb * S:(bb + 1) * S])
                b.qW.end()
            b.qW.drain(sync)

        @blk.tensor
        def _(tensor):
            for u in range(NU):
                tensor.wait_ge(*tok_in[u])
                for h in range(2):
                    hb = h * 64
                    for kc in range(16):
                        p = vt_pos[(u, h, kc)]
                        if p >= 2:
                            tensor.wait_ge(b.sE.h, so[("vte",) + vt_seq[p - 2]])
                        i = tensor.transpose(
                            b.psum[3 + (kc % 2)][0:128, 0:64],
                            b.stg[hb:hb + 64, VO + kc * 128:VO + kc * 128 + 128],
                            b.ident2[hb:hb + 64, 0:64])
                        b.tM.inc(i)
                    for idx, (ti, chunks, mk) in enumerate(tiles):
                        nch = len(chunks)
                        pv = prev_tile(u, h, idx)
                        if pv is not None:
                            tensor.wait_ge(b.vE.h, vo[("v1",) + pv])
                        i = tensor.matmul(
                            b.psum[6][0:128, 0:nch * 128],
                            b.stg[hb:hb + 64, QO + ti * 128:QO + ti * 128 + 128],
                            b.stg[hb:hb + 64,
                                  KO + chunks[0] * 128:KO + chunks[0] * 128 + nch * 128],
                            start=True, stop=True)
                        b.tM.inc(i)
                        tensor.wait_ge(b.vE.h, vo[("v4b", u, h, ti)])
                        for j in range(nch):
                            i = tensor.transpose(
                                b.psum[7][0:128, j * 128:j * 128 + 128],
                                b.wst[:, PRN + j * 128:PRN + j * 128 + 128],
                                b.ident[:, :])
                        b.tM.inc(i)
                        tensor.wait_ge(b.vE.h, vo[("v5b", u, h, ti)])
                        tensor.wait_ge(b.sE.h, so[("vte", u, h, 15)])
                        if pv is not None:
                            tensor.wait_ge(b.sE.h, so[("oev",) + pv])
                        for j in range(nch):
                            i = tensor.matmul(
                                b.psum[5][0:64, 0:128],
                                b.wst[:, VT + h * 1024 + chunks[j] * 64:
                                      VT + h * 1024 + chunks[j] * 64 + 64],
                                b.wst[:, PT + j * 128:PT + j * 128 + 128],
                                start=(j == 0), stop=(j == nch - 1))
                        b.tM.inc(i)

        @blk.vector
        def _(vector):
            for u in range(NU):
                for h in range(2):
                    for idx, (ti, chunks, mk) in enumerate(tiles):
                        nch = len(chunks)
                        pv = prev_tile(u, h, idx)
                        vector.wait_ge(b.tM.h, to[("qk", u, h, ti)])
                        if pv is not None:
                            vector.wait_ge(b.sE.h, so[("exp",) + pv])
                        b.vE.inc(vector.scalar_tensor_tensor(
                            b.wst[:, SCB:SCB + nch * 128],
                            b.psum[6][0:128, 0:nch * 128],
                            0.125, b.masks[mk][:, 0:nch * 128],
                            ALU.mult, ALU.add))
                        b.vE.inc(vector.tensor_reduce(
                            b.sc[:, C_NEGMAX:C_NEGMAX + 1],
                            b.wst[:, SCB:SCB + nch * 128],
                            AX.X, ALU.max, negate=True))
                        b.vE.inc(vector.tensor_copy(
                            b.sc[:, C_TMP2:C_TMP2 + 1],
                            b.sc[:, C_NEGMAX:C_NEGMAX + 1]))
                        vector.wait_ge(b.sE.h, so[("exp2", u, h, ti)])
                        b.vE.inc(vector.reciprocal(
                            b.sc[:, C_RCP:C_RCP + 1], b.sc[:, C_RSUM:C_RSUM + 1]))
                        vector.wait_ge(b.vE.h, vo[("v3", u, h, ti)])
                        b.vE.inc(vector.tensor_scalar(
                            b.wst[:, PRN:PRN + nch * 128],
                            b.wst[:, PR:PR + nch * 128],
                            b.sc[:, C_RCP:C_RCP + 1], None, ALU.mult))
                        b.vE.inc(vector.tensor_copy(
                            b.sc[:, C_TMP2:C_TMP2 + 1],
                            b.wst[:, PRN + nch * 128 - 1:PRN + nch * 128]))
                        vector.wait_ge(b.tM.h, to[("pnt", u, h, ti)])
                        b.vE.inc(vector.tensor_copy(
                            b.wst[:, PT:PT + nch * 128],
                            b.psum[7][0:128, 0:nch * 128]))
                        b.vE.inc(vector.tensor_copy(
                            b.sc[:, C_TMP2:C_TMP2 + 1],
                            b.wst[:, PT + nch * 128 - 1:PT + nch * 128]))

        @blk.scalar
        def _(scalar):
            for u in range(NU):
                for h in range(2):
                    for kc in range(16):
                        scalar.wait_ge(b.tM.h, to[("vt", u, h, kc)])
                        if u >= 1 and kc == 0:
                            scalar.wait_ge(b.tM.h, to[("av", u - 1, h, last_ti)])
                        b.sE.inc(scalar.copy(
                            b.wst[0:128, VT + h * 1024 + kc * 64:VT + h * 1024 + kc * 64 + 64],
                            b.psum[3 + (kc % 2)][0:128, 0:64]))
                    for idx, (ti, chunks, mk) in enumerate(tiles):
                        nch = len(chunks)
                        pv = prev_tile(u, h, idx)
                        scalar.wait_ge(b.vE.h, vo[("v2b", u, h, ti)])
                        if pv is not None:
                            scalar.wait_ge(b.tM.h, to[("pnt",) + pv])
                        b.sE.inc(scalar.activation(
                            b.wst[:, PR:PR + nch * 128],
                            b.wst[:, SCB:SCB + nch * 128],
                            AF.Exp, bias=b.sc[:, C_NEGMAX:C_NEGMAX + 1], scale=1.0,
                            accum_out=b.sc[:, C_RSUM:C_RSUM + 1]))
                        b.sE.inc(scalar.copy(
                            b.sc[:, C_TMP:C_TMP + 1], b.sc[:, C_RSUM:C_RSUM + 1]))
                        scalar.wait_ge(b.tM.h, to[("av", u, h, ti)])
                        if u >= 1 and h == 0 and idx == 0:
                            scalar.wait_ge(*tok_st[u - 1])
                        b.sE.inc(scalar.copy(
                            b.stg[h * 64:h * 64 + 64, OO + ti * 128:OO + ti * 128 + 128],
                            b.psum[5][0:64, 0:128]))

        @blk.gpsimd
        def _(gpsimd):
            for u in range(NU):
                bb, hp = divmod(u, 8)
                gpsimd.wait_ge(b.sE.h, so[("oev", u, 1, last_ti)])
                b.qX.begin(gpsimd)
                b.qX.dma(gpsimd,
                         b.outTb.ap()[hp * 128:hp * 128 + 128, bb * S:(bb + 1) * S],
                         b.stg[:, OO:OO + 2048])
                b.qX.end()
            b.qX.drain(gpsimd)


def _ph_attn0(b):
    _emit_attn(b, "attn0", l0_tiles())


def _ph_attn1(b):
    _emit_attn(b, "attn1", l1_tiles())


def _emit_oproj(b, name, w_dram, x_in_dram, x_out_dram, add_arena=False):
    nc = b.nc
    t0, v0 = b.tM.v, b.vE.v
    tok_in = [b.qW.plan_granule(2 * DC) for _ in range(NSL)]
    tok_w = [b.qS.plan_granule(DC) for _ in range(NSL * DC)]
    tok_st = [b.qX.plan_granule(1) for _ in range(NSL * DC)]
    with nc.Block(name) as blk:
        @blk.sync
        def _(sync):
            for sl in range(NSL):
                if sl >= 1:
                    sync.wait_ge(*tok_st[sl * DC - 1])
                b.qW.begin(sync)
                for dc in range(DC):
                    b.qW.dma(sync, b.stg[:, dc * 512:dc * 512 + 512],
                             b.outTb.ap()[dc * 128:(dc + 1) * 128, sl * SL:(sl + 1) * SL])
                for ec in range(DC):
                    b.qW.dma(sync, b.stg[:, 4096 + ec * 512:4096 + ec * 512 + 512],
                             x_in_dram.ap()[ec * 128:(ec + 1) * 128, sl * SL:(sl + 1) * SL])
                b.qW.end()
            b.qW.drain(sync)

        @blk.scalar
        def _(scalar):
            for sl in range(NSL):
                for ec in range(DC):
                    k = sl * DC + ec
                    slot = (k % 2) * 1024
                    if k >= 2:
                        scalar.wait_ge(b.tM.h, t0 + k - 1)
                    b.qS.begin(scalar)
                    for dc in range(DC):
                        b.qS.dma(scalar,
                                 b.wst[:, slot + dc * 128:slot + dc * 128 + 128],
                                 w_dram.ap()[dc * 128:(dc + 1) * 128,
                                             ec * 128:(ec + 1) * 128])
                    b.qS.end()
            b.qS.drain(scalar)

        @blk.tensor
        def _(tensor):
            for sl in range(NSL):
                tensor.wait_ge(*tok_in[sl])
                for ec in range(DC):
                    k = sl * DC + ec
                    slot = (k % 2) * 1024
                    pm = b.psum[k % 4]
                    tensor.wait_ge(*tok_w[k])
                    if k >= 4:
                        tensor.wait_ge(b.vE.h, v0 + (2 if add_arena else 1) * (k - 3))
                    for dc in range(DC):
                        i = tensor.matmul(
                            pm[:, :],
                            b.wst[:, slot + dc * 128:slot + dc * 128 + 128],
                            b.stg[:, dc * 512:dc * 512 + 512],
                            start=(dc == 0), stop=(dc == DC - 1))
                    b.tM.inc(i)

        @blk.vector
        def _(vector):
            nvk = 2 if add_arena else 1
            for k in range(NSL * DC):
                pm = b.psum[k % 4]
                sl = k // DC
                ec = k % DC
                xs = b.stg[:, 4096 + ec * 512:4096 + ec * 512 + 512]
                vector.wait_ge(b.tM.h, t0 + k + 1)
                b.vE.inc(vector.tensor_tensor(xs, xs, pm[:, :], ALU.add))
                if add_arena:
                    b.vE.inc(vector.tensor_tensor(
                        xs, xs, b.xn[:, ec * T + sl * SL:ec * T + sl * SL + SL],
                        ALU.add))

        @blk.gpsimd
        def _(gpsimd):
            nvk = 2 if add_arena else 1
            for sl in range(NSL):
                for ec in range(DC):
                    k = sl * DC + ec
                    xs = b.stg[:, 4096 + ec * 512:4096 + ec * 512 + 512]
                    gpsimd.wait_ge(b.vE.h, v0 + nvk * (k + 1))
                    b.qX.begin(gpsimd)
                    b.qX.dma(gpsimd,
                             x_out_dram.ap()[ec * 128:(ec + 1) * 128, sl * SL:(sl + 1) * SL],
                             xs)
                    b.qX.end()
            b.qX.drain(gpsimd)


def _ph_oproj0(b):
    _emit_oproj(b, "oproj0", b.WoT, b.x0, b.x1b)


def _ph_oproj1(b):
    _emit_oproj(b, "oproj1", b.woutT, b.x2b, b.x3b, add_arena=True)


def _emit_gmax(b, name, cols, ccout, scale_emit):
    nc = b.nc
    tokz = b.qX.plan_granule(1)
    toks = [b.qX.plan_granule(1) for _ in cols]
    tok_bc = [b.qX.plan_granule(1) for _ in cols]
    with nc.Block(name) as blk:
        @blk.tensor
        def _(tensor):
            for i, (col, slot) in enumerate(cols):
                it = tensor.transpose(
                    b.psum[6][0:1, i * 128:i * 128 + 128],
                    b.sc[:, col:col + 1], b.ident[:, :])
                b.tM.inc(it)

        @blk.vector
        def _(vector):
            vector.wait_ge(b.tM.h, b.tM.v)
            for i, (col, slot) in enumerate(cols):
                b.vE.inc(vector.tensor_reduce(
                    b.sc[0:1, C_TMP + i:C_TMP + i + 1],
                    b.psum[6][0:1, i * 128:i * 128 + 128], AX.X, ALU.max))

        @blk.gpsimd
        def _(gpsimd):
            gpsimd.wait_ge(b.vE.h, b.vE.v)
            b.qX.begin(gpsimd)
            b.qX.dma(gpsimd, b.ccin.ap(), b.sc[0:1, C_ZERO8:C_ZERO8 + 8])
            b.qX.end()
            gpsimd.wait_ge(*tokz)
            for i, (col, slot) in enumerate(cols):
                b.qX.begin(gpsimd)
                b.qX.dma(gpsimd, b.ccin.ap()[0:1, slot:slot + 1],
                         b.sc[0:1, C_TMP + i:C_TMP + i + 1])
                b.qX.end()
                gpsimd.wait_ge(*toks[i])
            b.cc.inc(gpsimd.collective_compute(
                "AllReduce", ALU.max,
                replica_groups=[list(range(8))],
                ins=[b.ccin.ap().opt()],
                outs=[ccout.ap().opt()]))
            gpsimd.wait_ge(b.cc.h, b.cc.v)
            for i, (col, slot) in enumerate(cols):
                b.qX.begin(gpsimd)
                b.qX.dma(gpsimd, b.sc[:, col:col + 1],
                         ccout.ap()[0:1, slot:slot + 1].broadcast_to([128, 1]))
                b.qX.end()
            b.qX.drain(gpsimd)

    with nc.Block(name + "_s") as blk:
        @blk.vector
        def _(vector):
            scale_emit(vector)


def _ph_gmax0(b):
    def emit(v):
        b.vE.inc(v.tensor_scalar(b.sc[:, C_SX:C_SX + 1], b.sc[:, C_AMAX:C_AMAX + 1],
                                 1.0 / QMAX, 1e-12, ALU.mult, ALU.max))
        vfence(v, b)
        b.vE.inc(v.reciprocal(b.sc[:, C_ISX:C_ISX + 1], b.sc[:, C_SX:C_SX + 1]))
    _emit_gmax(b, "gmax0", [(C_AMAX, 0)], b.ccout[0], emit)


def _ph_gmax1(b):
    def emit(v):
        b.vE.inc(v.tensor_scalar(b.sc[:, C_SG:C_SG + 1], b.sc[:, C_AMG:C_AMG + 1],
                                 1.0 / QMAX, 1e-12, ALU.mult, ALU.max))
        b.vE.inc(v.tensor_scalar(b.sc[:, C_SU:C_SU + 1], b.sc[:, C_AMU:C_AMU + 1],
                                 1.0 / QMAX, 1e-12, ALU.mult, ALU.max))
        vfence(v, b)
        b.vE.inc(v.reciprocal(b.sc[:, C_ISG:C_ISG + 1], b.sc[:, C_SG:C_SG + 1]))
        vfence(v, b)
        b.vE.inc(v.reciprocal(b.sc[:, C_ISU:C_ISU + 1], b.sc[:, C_SU:C_SU + 1]))
    _emit_gmax(b, "gmax1", [(C_AMG, 0), (C_AMU, 1)], b.ccout[1], emit)


def _ph_gmax2(b):
    def emit(v):
        b.vE.inc(v.tensor_scalar(b.sc[:, C_SG2:C_SG2 + 1], b.sc[:, C_AMG2:C_AMG2 + 1],
                                 (2.0 ** -16) / QMAX, 1e-12, ALU.mult, ALU.max))
        vfence(v, b)
        b.vE.inc(v.reciprocal(b.sc[:, C_TMP:C_TMP + 1], b.sc[:, C_SG2:C_SG2 + 1]))
        vfence(v, b)
        b.vE.inc(v.tensor_scalar(b.sc[:, C_CMUL:C_CMUL + 1], b.sc[:, C_TMP:C_TMP + 1],
                                 2.0 ** -16, None, ALU.mult))
    _emit_gmax(b, "gmax2", [(C_AMG2, 0)], b.ccout[2], emit)


def _ph_gmax3(b):
    def emit(v):
        b.vE.inc(v.tensor_scalar(b.sc[:, C_SO:C_SO + 1], b.sc[:, C_AMO:C_AMO + 1],
                                 1.0 / QMAX, 1e-12, ALU.mult, ALU.max))
        vfence(v, b)
        b.vE.inc(v.reciprocal(b.sc[:, C_ISO:C_ISO + 1], b.sc[:, C_SO:C_SO + 1]))
    _emit_gmax(b, "gmax3", [(C_AMO, 0)], b.ccout[3], emit)


def _ph_quant0(b):
    nc = b.nc
    with nc.Block("quant0") as blk:
        @blk.vector
        def _(vector):
            for dc in range(DC):
                src = b.xn[:, dc * T:(dc + 1) * T]
                t = b.stg[:, 0:T]
                b.vE.inc(vector.tensor_scalar(
                    t, src, b.sc[:, C_ISX:C_ISX + 1], MAGIC, ALU.mult, ALU.add))
                b.vE.inc(vector.tensor_scalar(
                    src, t, -MAGIC, b.sc[:, C_SX:C_SX + 1], ALU.add, ALU.mult))


def _ph_gup(b):
    nc = b.nc
    t0, v0, s0 = b.tM.v, b.vE.v, b.sE.v
    units = [(b.w_gate, b.gateb, C_AMG), (b.w_up, b.upb, C_AMU)]
    NU = 48
    tok_w = [b.qW.plan_granule(1) for _ in range(NU)]
    tok_st = [b.qX.plan_granule(1) for _ in range(NU * NSL)]

    to, vo, so = {}, {}, {}
    nt, nv, ns = t0, v0, s0
    for u in range(NU):
        for ev in ("red", "s", "rs", "rep", "mul", "rnd", "srep", "dq"):
            nv += 1
            vo[(ev, u)] = nv
        for dc in range(DC):
            nt += 1
            to[("tr", u, dc)] = nt
            ns += 1
            so[("te", u, dc)] = ns
        for sl in range(NSL):
            nt += 1
            to[("mm", u, sl)] = nt
            nv += 1
            vo[("ev", u, sl)] = nv
            nv += 1
            vo[("am", u, sl)] = nv
            if not (u % 24 == 0 and sl == 0):
                nv += 1
                vo[("amx", u, sl)] = nv

    def last_v(u, sl):
        return vo[("amx", u, sl)] if ("amx", u, sl) in vo else vo[("am", u, sl)]

    with nc.Block("gup") as blk:
        @blk.sync
        def _(sync):
            for u in range(NU):
                wdram = units[u // 24][0]
                ic = u % 24
                slot = (u % 2) * 1024
                if u >= 2:
                    sync.wait_ge(b.vE.h, vo[("mul", u - 2)])
                b.qW.begin(sync)
                b.qW.dma(sync, b.wst[:, slot:slot + 1024],
                         wdram.ap()[ic * 128:(ic + 1) * 128, 0:1024])
                b.qW.end()
            b.qW.drain(sync)

        @blk.vector
        def _(vector):
            for u in range(NU):
                slot = (u % 2) * 1024
                raw = b.wst[:, slot:slot + 1024]
                raw3 = raw.rearrange("p (c k) -> p c k", k=32)
                ab = b.sc[:, C_BLK:C_BLK + 32]
                scl = b.sc[:, C_BLK + 32:C_BLK + 64]
                rs = b.sc[:, C_BLK + 64:C_BLK + 96]
                rep = b.stg[:, 6144:7168]
                wq = b.stg[:, 7168:8192]
                wq2 = b.stg[:, 5120:6144]
                vector.wait_ge(*tok_w[u])
                if u >= 1:
                    vector.wait_ge(b.tM.h, to[("tr", u - 1, DC - 1)])
                b.vE.inc(vector.tensor_reduce(ab, raw3, AX.X, ALU.max,
                                              apply_absolute_value=True))
                vfence(vector, b)
                b.vE.inc(vector.tensor_scalar(scl, ab, 1.0 / 7.0, 1e-12,
                                              ALU.mult, ALU.max))
                vfence(vector, b)
                b.vE.inc(vector.reciprocal(rs, scl))
                vfence(vector, b)
                b.vE.inc(vector.tensor_copy(
                    rep.rearrange("p (c k) -> p c k", k=32),
                    rs.unsqueeze(2).broadcast_to([128, 32, 32])))
                b.vE.inc(vector.tensor_tensor(wq, raw, rep, ALU.mult))
                b.vE.inc(vector.tensor_scalar(wq2, wq, MAGIC, -MAGIC, ALU.add, ALU.add))
                b.vE.inc(vector.tensor_copy(
                    rep.rearrange("p (c k) -> p c k", k=32),
                    scl.unsqueeze(2).broadcast_to([128, 32, 32])))
                b.vE.inc(vector.tensor_tensor(wq, wq2, rep, ALU.mult))
                amc = units[u // 24][2]
                for sl in range(NSL):
                    k = u * NSL + sl
                    pm = b.psum[k % 4]
                    st = b.stg[:, (k % 4) * 512:(k % 4) * 512 + 512]
                    vector.wait_ge(b.tM.h, to[("mm", u, sl)])
                    if k >= 4:
                        vector.wait_ge(*tok_st[k - 4])
                    b.vE.inc(vector.tensor_copy(st, pm[:, :]))
                    first = (u % 24 == 0 and sl == 0)
                    am = b.sc[:, amc:amc + 1]
                    amt = b.sc[:, C_AMAXT:C_AMAXT + 1]
                    b.vE.inc(vector.tensor_reduce(
                        am if first else amt, st, AX.X, ALU.max,
                        apply_absolute_value=True))
                    if not first:
                        vfence(vector, b)
                        b.vE.inc(vector.tensor_tensor(am, am, amt, ALU.max))

        @blk.tensor
        def _(tensor):
            for u in range(NU):
                slot2 = 2048 + (u % 2) * 1024
                tensor.wait_ge(b.vE.h, vo[("dq", u)])
                for dc in range(DC):
                    pmT = b.psum[4 + (dc % 2)]
                    p = u * DC + dc
                    if p >= 2:
                        pu, pd = divmod(p - 2, DC)
                        tensor.wait_ge(b.sE.h, so[("te", pu, pd)])
                    i = tensor.transpose(
                        pmT[0:128, 0:128],
                        b.stg[:, 7168 + dc * 128:7168 + dc * 128 + 128],
                        b.ident[:, :])
                    b.tM.inc(i)
                tensor.wait_ge(b.sE.h, so[("te", u, DC - 1)])
                for sl in range(NSL):
                    k = u * NSL + sl
                    pm = b.psum[k % 4]
                    if k >= 4:
                        tensor.wait_ge(b.vE.h, last_v(*divmod(k - 4, NSL)))
                    for dc in range(DC):
                        i = tensor.matmul(
                            pm[:, :],
                            b.wst[:, slot2 + dc * 128:slot2 + dc * 128 + 128],
                            b.xn[:, dc * T + sl * SL:dc * T + sl * SL + SL],
                            start=(dc == 0), stop=(dc == DC - 1))
                    b.tM.inc(i)

        @blk.scalar
        def _(scalar):
            for u in range(NU):
                slot2 = 2048 + (u % 2) * 1024
                for dc in range(DC):
                    scalar.wait_ge(b.tM.h, to[("tr", u, dc)])
                    if u >= 2 and dc == 0:
                        scalar.wait_ge(b.tM.h, to[("mm", u - 2, NSL - 1)])
                    b.sE.inc(scalar.copy(
                        b.wst[:, slot2 + dc * 128:slot2 + dc * 128 + 128],
                        b.psum[4 + (dc % 2)][0:128, 0:128]))

        @blk.gpsimd
        def _(gpsimd):
            for u in range(NU):
                dst = units[u // 24][1]
                ic = u % 24
                for sl in range(NSL):
                    st = b.stg[:, ((u * NSL + sl) % 4) * 512:((u * NSL + sl) % 4) * 512 + 512]
                    gpsimd.wait_ge(b.vE.h, last_v(u, sl))
                    b.qX.begin(gpsimd)
                    b.qX.dma(gpsimd,
                             dst.ap()[ic * 128:(ic + 1) * 128, sl * SL:(sl + 1) * SL], st)
                    b.qX.end()
            b.qX.drain(gpsimd)


def _ph_sigch(b):
    nc = b.nc
    v0, s0 = b.vE.v, b.sE.v
    tok_w = [b.qW.plan_granule(1) for _ in range(24)]
    tok_st = [b.qX.plan_granule(1) for _ in range(24)]
    vo, so = {}, {}
    nv, ns = v0, s0
    for u in range(24):
        for ev in ("t", "gq"):
            nv += 1
            vo[(ev, u)] = nv
        ns += 1
        so[("sig", u)] = ns
        for ev in ("t2", "u16", "v", "am"):
            nv += 1
            vo[(ev, u)] = nv
        if u > 0:
            nv += 1
            vo[("amx", u)] = nv

    with nc.Block("sigch") as blk:
        @blk.sync
        def _(sync):
            for u in range(24):
                if u >= 2:
                    sync.wait_ge(b.vE.h, vo[("t", u - 1)])
                b.qW.begin(sync)
                b.qW.dma(sync, b.stg[:, (u % 2) * 4096:(u % 2) * 4096 + 4096],
                         b.gateb.ap()[u * 128:(u + 1) * 128, 0:T])
                b.qW.end()
            b.qW.drain(sync)

        @blk.vector
        def _(vector):
            for u in range(24):
                gate = b.stg[:, (u % 2) * 4096:(u % 2) * 4096 + 4096]
                t = b.xn[:, 0:4096]
                gq = b.xn[:, 4096:8192]
                sig = b.xn[:, 8192:12288]
                vout = b.xn[:, 12288:16384]
                vector.wait_ge(*tok_w[u])
                b.vE.inc(vector.tensor_scalar(
                    t, gate, b.sc[:, C_ISG:C_ISG + 1], MAGIC, ALU.mult, ALU.add))
                b.vE.inc(vector.tensor_scalar(
                    gq, t, -MAGIC, b.sc[:, C_SG:C_SG + 1], ALU.add, ALU.mult))
                vector.wait_ge(b.sE.h, so[("sig", u)])
                u16 = b.xn[:, 16384:20480]
                b.vE.inc(vector.tensor_scalar(
                    t, sig, 65536.0, MAGIC, ALU.mult, ALU.add))
                b.vE.inc(vector.tensor_scalar(
                    u16, t, -MAGIC, 65535.0, ALU.add, ALU.min))
                if u >= 1:
                    vector.wait_ge(*tok_st[u - 1])
                b.vE.inc(vector.tensor_tensor(vout, gq, u16, ALU.mult))
                first = (u == 0)
                am = b.sc[:, C_AMG2:C_AMG2 + 1]
                amt = b.sc[:, C_AMAXT:C_AMAXT + 1]
                b.vE.inc(vector.tensor_reduce(am if first else amt, vout, AX.X,
                                              ALU.max, apply_absolute_value=True))
                if not first:
                    vfence(vector, b)
                    b.vE.inc(vector.tensor_tensor(am, am, amt, ALU.max))

        @blk.scalar
        def _(scalar):
            for u in range(24):
                gq = b.xn[:, 4096:8192]
                sig = b.xn[:, 8192:12288]
                scalar.wait_ge(b.vE.h, vo[("gq", u)])
                b.sE.inc(scalar.activation(sig, gq, AF.Sigmoid))

        @blk.gpsimd
        def _(gpsimd):
            for u in range(24):
                vout = b.xn[:, 12288:16384]
                gpsimd.wait_ge(b.vE.h, vo[("v", u)])
                b.qX.begin(gpsimd)
                b.qX.dma(gpsimd, b.g2vb.ap()[u * 128:(u + 1) * 128, 0:T], vout)
                b.qX.end()
            b.qX.drain(gpsimd)


def _ph_omul(b):
    nc = b.nc
    v0 = b.vE.v
    tok_w = [b.qW.plan_granule(2) for _ in range(24)]
    tok_st = [b.qX.plan_granule(1) for _ in range(24)]
    vo = {}
    nv = v0
    for u in range(24):
        for ev in ("t", "upq", "t2", "g2q", "o", "am"):
            nv += 1
            vo[(ev, u)] = nv
        if u > 0:
            nv += 1
            vo[("amx", u)] = nv

    with nc.Block("omul") as blk:
        @blk.sync
        def _(sync):
            for u in range(24):
                if u >= 1:
                    sync.wait_ge(b.vE.h, vo[("o", u - 1)])
                b.qW.begin(sync)
                b.qW.dma(sync, b.stg[:, 0:4096],
                         b.upb.ap()[u * 128:(u + 1) * 128, 0:T])
                b.qW.dma(sync, b.stg[:, 4096:8192],
                         b.g2vb.ap()[u * 128:(u + 1) * 128, 0:T])
                b.qW.end()
            b.qW.drain(sync)

        @blk.vector
        def _(vector):
            for u in range(24):
                up = b.stg[:, 0:4096]
                vv = b.stg[:, 4096:8192]
                t = b.xn[:, 0:4096]
                upq = b.xn[:, 4096:8192]
                g2q = b.xn[:, 8192:12288]
                o = b.xn[:, 12288:16384]
                vector.wait_ge(*tok_w[u])
                b.vE.inc(vector.tensor_scalar(
                    t, up, b.sc[:, C_ISU:C_ISU + 1], MAGIC, ALU.mult, ALU.add))
                b.vE.inc(vector.tensor_scalar(
                    upq, t, -MAGIC, b.sc[:, C_SU:C_SU + 1], ALU.add, ALU.mult))
                b.vE.inc(vector.tensor_scalar(
                    t, vv, b.sc[:, C_CMUL:C_CMUL + 1], MAGIC, ALU.mult, ALU.add))
                b.vE.inc(vector.tensor_scalar(
                    g2q, t, -MAGIC, b.sc[:, C_SG2:C_SG2 + 1], ALU.add, ALU.mult))
                if u >= 1:
                    vector.wait_ge(*tok_st[u - 1])
                b.vE.inc(vector.tensor_tensor(o, g2q, upq, ALU.mult))
                first = (u == 0)
                am = b.sc[:, C_AMO:C_AMO + 1]
                amt = b.sc[:, C_AMAXT:C_AMAXT + 1]
                b.vE.inc(vector.tensor_reduce(am if first else amt, o, AX.X,
                                              ALU.max, apply_absolute_value=True))
                if not first:
                    vfence(vector, b)
                    b.vE.inc(vector.tensor_tensor(am, am, amt, ALU.max))

        @blk.gpsimd
        def _(gpsimd):
            for u in range(24):
                o = b.xn[:, 12288:16384]
                gpsimd.wait_ge(b.vE.h, vo[("am", u)] if u == 0 else vo[("amx", u)])
                b.qX.begin(gpsimd)
                b.qX.dma(gpsimd, b.gateb.ap()[u * 128:(u + 1) * 128, 0:T], o)
                b.qX.end()
            b.qX.drain(gpsimd)


def _ph_wdqdown(b):
    nc = b.nc
    t0, v0, s0 = b.tM.v, b.vE.v, b.sE.v
    tok_w = [b.qW.plan_granule(1) for _ in range(DC)]
    to, vo, so = {}, {}, {}
    nt, nv, ns = t0, v0, s0
    for oc in range(DC):
        for ev in ("red", "s", "rs", "rep", "mul", "rnd", "srep", "dq"):
            nv += 1
            vo[(ev, oc)] = nv
        for ic in range(24):
            nt += 1
            to[("tr", oc, ic)] = nt
            ns += 1
            so[("te", oc, ic)] = ns

    with nc.Block("wdqdown") as blk:
        @blk.sync
        def _(sync):
            for oc in range(DC):
                if oc >= 1:
                    sync.wait_ge(b.vE.h, vo[("dq", oc - 1)])
                b.qW.begin(sync)
                b.qW.dma(sync, b.stg[:, 0:3072],
                         b.w_down.ap()[oc * 128:(oc + 1) * 128, 0:3072])
                b.qW.end()
            b.qW.drain(sync)

        @blk.vector
        def _(vector):
            for oc in range(DC):
                raw = b.stg[:, 0:3072]
                raw3 = raw.rearrange("p (c k) -> p c k", k=32)
                ab = b.sc[:, C_BLK:C_BLK + 96]
                scl = b.sc[:, C_BLK + 96:C_BLK + 192]
                rep = b.stg[:, 4096:7168]
                wq = b.wst[:, 0:3072]
                vector.wait_ge(*tok_w[oc])
                if oc >= 1:
                    vector.wait_ge(b.tM.h, to[("tr", oc - 1, 23)])
                b.vE.inc(vector.tensor_reduce(ab, raw3, AX.X, ALU.max,
                                              apply_absolute_value=True))
                vfence(vector, b)
                b.vE.inc(vector.tensor_scalar(scl, ab, 1.0 / 7.0, 1e-12,
                                              ALU.mult, ALU.max))
                vfence(vector, b)
                b.vE.inc(vector.reciprocal(ab, scl))
                vfence(vector, b)
                b.vE.inc(vector.tensor_copy(
                    rep.rearrange("p (c k) -> p c k", k=32),
                    ab.unsqueeze(2).broadcast_to([128, 96, 32])))
                b.vE.inc(vector.tensor_tensor(wq, raw, rep, ALU.mult))
                b.vE.inc(vector.tensor_scalar(raw, wq, MAGIC, -MAGIC, ALU.add, ALU.add))
                b.vE.inc(vector.tensor_copy(
                    rep.rearrange("p (c k) -> p c k", k=32),
                    scl.unsqueeze(2).broadcast_to([128, 96, 32])))
                b.vE.inc(vector.tensor_tensor(wq, raw, rep, ALU.mult))

        @blk.tensor
        def _(tensor):
            for oc in range(DC):
                tensor.wait_ge(b.vE.h, vo[("dq", oc)])
                for ic in range(24):
                    p = oc * 24 + ic
                    pmT = b.psum[4 + (ic % 2)]
                    if p >= 2:
                        po, pi = divmod(p - 2, 24)
                        tensor.wait_ge(b.sE.h, so[("te", po, pi)])
                    i = tensor.transpose(
                        pmT[0:128, 0:128],
                        b.wst[:, ic * 128:ic * 128 + 128],
                        b.ident[:, :])
                    b.tM.inc(i)

        @blk.scalar
        def _(scalar):
            for oc in range(DC):
                for ic in range(24):
                    scalar.wait_ge(b.tM.h, to[("tr", oc, ic)])
                    b.sE.inc(scalar.copy(
                        b.xn[:, (ic * 8 + oc) * 128:(ic * 8 + oc) * 128 + 128],
                        b.psum[4 + (ic % 2)][0:128, 0:128]))


def _ph_down(b):
    nc = b.nc
    NSH = 16
    SH = 256
    t0, v0 = b.tM.v, b.vE.v
    tok_in = [b.qW.plan_granule(32) for _ in range(NSH)]
    tok_st = [b.qX.plan_granule(1) for _ in range(NSH * DC)]
    to, vo = {}, {}
    nt, nv = t0, v0
    for sh in range(NSH):
        for ic in range(24):
            nv += 2
            vo[("q", sh, ic)] = nv
        for ec in range(DC):
            nt += 1
            to[("mm", sh, ec)] = nt
            nv += 1
            vo[("add", sh, ec)] = nv

    with nc.Block("down") as blk:
        @blk.sync
        def _(sync):
            for sh in range(NSH):
                if sh >= 1:
                    sync.wait_ge(b.vE.h, vo[("add", sh - 1, DC - 1)])
                    sync.wait_ge(*tok_st[sh * DC - 1])
                b.qW.begin(sync)
                for ic in range(24):
                    b.qW.dma(sync, b.stg[:, ic * 256:ic * 256 + 256],
                             b.gateb.ap()[ic * 128:(ic + 1) * 128, sh * SH:(sh + 1) * SH])
                for ec in range(DC):
                    b.qW.dma(sync, b.stg[:, 6144 + ec * 256:6144 + ec * 256 + 256],
                             b.x1b.ap()[ec * 128:(ec + 1) * 128, sh * SH:(sh + 1) * SH])
                b.qW.end()
            b.qW.drain(sync)

        @blk.vector
        def _(vector):
            for sh in range(NSH):
                vector.wait_ge(*tok_in[sh])
                for ic in range(24):
                    oq = b.stg[:, ic * 256:ic * 256 + 256]
                    t = b.wst[:, 0:256]
                    b.vE.inc(vector.tensor_scalar(
                        t, oq, b.sc[:, C_ISO:C_ISO + 1], MAGIC, ALU.mult, ALU.add))
                    b.vE.inc(vector.tensor_scalar(
                        oq, t, -MAGIC, b.sc[:, C_SO:C_SO + 1], ALU.add, ALU.mult))
                for ec in range(DC):
                    pm = b.psum[ec % 4]
                    xs = b.stg[:, 6144 + ec * 256:6144 + ec * 256 + 256]
                    vector.wait_ge(b.tM.h, to[("mm", sh, ec)])
                    b.vE.inc(vector.tensor_tensor(xs, xs, pm[0:128, 0:SH], ALU.add))

        @blk.tensor
        def _(tensor):
            for sh in range(NSH):
                tensor.wait_ge(b.vE.h, vo[("q", sh, 23)])
                for ec in range(DC):
                    k = sh * DC + ec
                    pm = b.psum[k % 4]
                    if k >= 4:
                        kk = k - 4
                        tensor.wait_ge(b.vE.h, vo[("add", kk // DC, kk % DC)])
                    for ic in range(24):
                        i = tensor.matmul(
                            pm[0:128, 0:SH],
                            b.xn[:, (ic * 8 + ec) * 128:(ic * 8 + ec) * 128 + 128],
                            b.stg[:, ic * 256:ic * 256 + 256],
                            start=(ic == 0), stop=(ic == 23))
                    b.tM.inc(i)

        @blk.gpsimd
        def _(gpsimd):
            for sh in range(NSH):
                for ec in range(DC):
                    xs = b.stg[:, 6144 + ec * 256:6144 + ec * 256 + 256]
                    gpsimd.wait_ge(b.vE.h, vo[("add", sh, ec)])
                    b.qX.begin(gpsimd)
                    b.qX.dma(gpsimd,
                             b.x2b.ap()[ec * 128:(ec + 1) * 128, sh * SH:(sh + 1) * SH],
                             xs)
                    b.qX.end()
            b.qX.drain(gpsimd)


def _ph_gate1(b):
    nc = b.nc
    t0, v0, s0 = b.tM.v, b.vE.v, b.sE.v
    tok_w = [b.qW.plan_granule(DC) for _ in range(DC)]
    to, vo, so = {}, {}, {}
    nt, nv, ns = t0, v0, s0
    for oc in range(DC):
        for sl in range(NSL):
            nt += 1
            to[("mm", oc, sl)] = nt
            ns += 1
            so[("relu", oc, sl)] = ns
            nv += 1
            vo[("red", oc, sl)] = nv
            if sl % 4 != 0:
                nv += 1
                vo[("acc", oc, sl)] = nv

    with nc.Block("gate1") as blk:
        @blk.sync
        def _(sync):
            for oc in range(DC):
                if oc >= 2:
                    sync.wait_ge(b.tM.h, to[("mm", oc - 1, NSL - 1)])
                slot = (oc % 2) * 1024
                b.qW.begin(sync)
                for dc in range(DC):
                    b.qW.dma(sync, b.wst[:, slot + dc * 128:slot + dc * 128 + 128],
                             b.gw1T.ap()[dc * 128:(dc + 1) * 128, oc * 128:(oc + 1) * 128])
                b.qW.end()
            b.qW.drain(sync)

        @blk.tensor
        def _(tensor):
            for oc in range(DC):
                slot = (oc % 2) * 1024
                tensor.wait_ge(*tok_w[oc])
                for sl in range(NSL):
                    k = oc * NSL + sl
                    pm = b.psum[k % 4]
                    if k >= 4:
                        kk = k - 4
                        tensor.wait_ge(b.sE.h, so[("relu", kk // NSL, kk % NSL)])
                    for dc in range(DC):
                        i = tensor.matmul(
                            pm[:, :],
                            b.wst[:, slot + dc * 128:slot + dc * 128 + 128],
                            b.xn[:, dc * T + sl * SL:dc * T + sl * SL + SL],
                            start=(dc == 0), stop=(dc == DC - 1))
                    b.tM.inc(i)

        @blk.scalar
        def _(scalar):
            for oc in range(DC):
                for sl in range(NSL):
                    k = oc * NSL + sl
                    scalar.wait_ge(b.tM.h, to[("mm", oc, sl)])
                    if k >= 2:
                        kk = k - 2
                        scalar.wait_ge(b.vE.h, vo[("red", kk // NSL, kk % NSL)])
                    b.sE.inc(scalar.activation(
                        b.stg[:, (k % 2) * 512:(k % 2) * 512 + 512],
                        b.psum[k % 4][:, :], AF.Relu,
                        bias=b.sc[:, C_GB1 + oc:C_GB1 + oc + 1], scale=1.0))

        @blk.vector
        def _(vector):
            for oc in range(DC):
                for sl in range(NSL):
                    k = oc * NSL + sl
                    bb = sl // 4
                    col = C_HB + bb * 8 + oc
                    vector.wait_ge(b.sE.h, so[("relu", oc, sl)])
                    first = (sl % 4 == 0)
                    dst = b.sc[:, col:col + 1] if first else b.sc[:, C_TMP:C_TMP + 1]
                    b.vE.inc(vector.tensor_reduce(
                        dst, b.stg[:, (k % 2) * 512:(k % 2) * 512 + 512],
                        AX.X, ALU.add))
                    if not first:
                        vfence(vector, b)
                        b.vE.inc(vector.tensor_tensor(
                            b.sc[:, col:col + 1], b.sc[:, col:col + 1],
                            b.sc[:, C_TMP:C_TMP + 1], ALU.add))


def _ph_gvec(b):
    nc = b.nc
    tok_a = b.qW.plan_granule(2 + DC)
    tok_g = b.qX.plan_granule(1)
    b.qX.plan_granule(2 * 8)
    with nc.Block("gvec") as blk:
        @blk.sync
        def _(sync):
            b.qW.begin(sync)
            b.qW.dma(sync, b.wst[0:8, 0:4096], b.Tb1.ap())
            b.qW.dma(sync, b.stg[0:8, 0:1024], b.Tb2.ap())
            for oc in range(DC):
                b.qW.dma(sync, b.stg[:, 1024 + oc * 8:1024 + oc * 8 + 8],
                         b.gw2T.ap()[oc * 128:(oc + 1) * 128, 0:8])
            b.qW.end()
            b.qW.drain(sync)

        @blk.tensor
        def _(tensor):
            tensor.wait_ge(*tok_a)
            for bb in range(NB):
                for oc in range(DC):
                    i = tensor.matmul(
                        b.psum[6][0:8, bb:bb + 1],
                        b.stg[:, 1024 + oc * 8:1024 + oc * 8 + 8],
                        b.sc[:, C_HB + bb * 8 + oc:C_HB + bb * 8 + oc + 1],
                        start=(oc == 0), stop=(oc == DC - 1))
                b.tM.inc(i)

        @blk.vector
        def _(vector):
            vector.wait_ge(b.tM.h, b.tM.v)
            for bb in range(NB):
                b.vE.inc(vector.tensor_scalar(
                    b.sc[0:8, C_G + bb:C_G + bb + 1],
                    b.psum[6][0:8, bb:bb + 1],
                    1.0 / S, b.sc[0:8, C_GB2:C_GB2 + 1], ALU.mult, ALU.add))
            vfence(vector, b)
            b.vE.inc(vector.tensor_copy(b.sc[:, C_TMP2:C_TMP2 + 1],
                                        b.sc[:, C_G:C_G + 1]))

        @blk.gpsimd
        def _(gpsimd):
            gpsimd.wait_ge(b.vE.h, b.vE.v)
            b.qX.begin(gpsimd)
            b.qX.dma(gpsimd, b.gdram.ap(), b.sc[0:8, C_G:C_G + 2])
            b.qX.end()
            gpsimd.wait_ge(*tok_g)
            b.qX.begin(gpsimd)
            for bb in range(NB):
                for e in range(8):
                    b.qX.dma(gpsimd,
                             b.sc[:, C_GBC + bb * 8 + e:C_GBC + bb * 8 + e + 1],
                             b.gdram.ap()[e:e + 1, bb:bb + 1].broadcast_to([128, 1]))
            b.qX.end()
            b.qX.drain(gpsimd)

    with nc.Block("gvec2") as blk:
        @blk.tensor
        def _(tensor):
            for bb in range(NB):
                for fc in range(32):
                    i = tensor.matmul(
                        b.psum[6][0:128, bb * 40 + fc:bb * 40 + fc + 1],
                        b.wst[0:8, fc * 128:fc * 128 + 128],
                        b.sc[0:8, C_G + bb:C_G + bb + 1],
                        start=True, stop=True)
                for hc in range(DC):
                    i = tensor.matmul(
                        b.psum[7][0:128, bb * 8 + hc:bb * 8 + hc + 1],
                        b.stg[0:8, hc * 128:hc * 128 + 128],
                        b.sc[0:8, C_G + bb:C_G + bb + 1],
                        start=True, stop=True)
            b.tM.inc(i)

        @blk.vector
        def _(vector):
            vector.wait_ge(b.tM.h, b.tM.v)
            for bb in range(NB):
                mb1 = b.sc[:, C_MB1[bb]:C_MB1[bb] + 32]
                b.vE.inc(vector.tensor_tensor(
                    mb1, b.psum[6][0:128, bb * 40:bb * 40 + 32],
                    b.sc[:, C_B1C:C_B1C + 32], ALU.add))
                mb2 = b.sc[:, C_MB2[bb]:C_MB2[bb] + 8]
                b.vE.inc(vector.tensor_tensor(
                    mb2, b.psum[7][0:128, bb * 8:bb * 8 + 8],
                    b.sc[:, C_B2C:C_B2C + 8], ALU.add))


def _ph_wmod(b):
    nc = b.nc
    v0 = b.vE.v
    tiles = []
    for bb in range(NB):
        for (wT, TWT, dstd, nrc, ncc) in ((b.W1T, b.TW1T, b.mW1Tb[bb], 8, 8),
                                          (b.W2T, b.TW2T, b.mW2Tb[bb], 32, 2)):
            for rc in range(nrc):
                for cc in range(ncc):
                    tiles.append((bb, wT, TWT, dstd, rc, cc))
    NT = len(tiles)
    tok_w = [b.qW.plan_granule(4) for _ in range(NT)]
    tok_s = [b.qS.plan_granule(5) for _ in range(NT)]
    tok_st = [b.qX.plan_granule(1) for _ in range(NT)]
    vo = {}
    nv = v0
    for k in range(NT):
        nv += 8
        vo[("acc", k)] = nv

    with nc.Block("wmod") as blk:
        @blk.sync
        def _(sync):
            for k, (bb, wT, TWT, dstd, rc, cc) in enumerate(tiles):
                slot = (k % 2) * 4096
                if k >= 2:
                    sync.wait_ge(b.vE.h, vo[("acc", k - 2)])
                b.qW.begin(sync)
                for e in range(4):
                    b.qW.dma(sync, b.stg[:, slot + e * 512:slot + e * 512 + 512],
                             TWT.ap()[e, rc * 128:(rc + 1) * 128, cc * 512:(cc + 1) * 512])
                b.qW.end()
            b.qW.drain(sync)

        @blk.scalar
        def _(scalar):
            for k, (bb, wT, TWT, dstd, rc, cc) in enumerate(tiles):
                slot = (k % 2) * 4096
                if k >= 2:
                    scalar.wait_ge(b.vE.h, vo[("acc", k - 2)])
                b.qS.begin(scalar)
                for e in range(4, 8):
                    b.qS.dma(scalar, b.stg[:, slot + e * 512:slot + e * 512 + 512],
                             TWT.ap()[e, rc * 128:(rc + 1) * 128, cc * 512:(cc + 1) * 512])
                b.qS.dma(scalar, b.wst[:, (k % 2) * 512:(k % 2) * 512 + 512],
                         wT.ap()[rc * 128:(rc + 1) * 128, cc * 512:(cc + 1) * 512])
                b.qS.end()
            b.qS.drain(scalar)

        @blk.vector
        def _(vector):
            for k, (bb, wT, TWT, dstd, rc, cc) in enumerate(tiles):
                slot = (k % 2) * 4096
                acc = b.wst[:, 1024 + (k % 2) * 512:1024 + (k % 2) * 512 + 512]
                vector.wait_ge(*tok_w[k])
                vector.wait_ge(*tok_s[k])
                if k >= 2:
                    vector.wait_ge(*tok_st[k - 2])
                for e in range(8):
                    tw = b.stg[:, slot + e * 512:slot + e * 512 + 512]
                    gc = b.sc[:, C_GBC + bb * 8 + e:C_GBC + bb * 8 + e + 1]
                    src = b.wst[:, (k % 2) * 512:(k % 2) * 512 + 512] if e == 0 else acc
                    b.vE.inc(vector.scalar_tensor_tensor(
                        acc, tw, gc, src, ALU.mult, ALU.add))

        @blk.gpsimd
        def _(gpsimd):
            for k, (bb, wT, TWT, dstd, rc, cc) in enumerate(tiles):
                acc = b.wst[:, 1024 + (k % 2) * 512:1024 + (k % 2) * 512 + 512]
                gpsimd.wait_ge(b.vE.h, vo[("acc", k)])
                b.qX.begin(gpsimd)
                b.qX.dma(gpsimd,
                         dstd.ap()[rc * 128:(rc + 1) * 128, cc * 512:(cc + 1) * 512],
                         acc)
                b.qX.end()
            b.qX.drain(gpsimd)


def _ph_ffnmain(b):
    nc = b.nc
    t0, v0, s0 = b.tM.v, b.vE.v, b.sE.v
    to, vo, so = {}, {}, {}
    tok_in, tok_w1, tok_w2, tok_y = {}, {}, {}, {}
    nt, nv, ns = t0, v0, s0
    for bb in range(NB):
        for sl in range(4):
            u = bb * 4 + sl
            tok_in[u] = b.qW.plan_granule(16)
            for fc in range(32):
                tok_w1[(u, fc)] = b.qW.plan_granule(DC)
            for fc in range(32):
                nt += 1
                to[("mm1", bb, sl, fc)] = nt
                ns += 1
                so[("gelu", bb, sl, fc)] = ns
            for ec in range(DC):
                for fg in range(4):
                    tok_w2[(u, ec, fg)] = b.qX.plan_granule(8)
                    nt += 1
                    to[("mm2g", bb, sl, ec, fg)] = nt
                ns += 1
                so[("cp", bb, sl, ec)] = ns
                nv += 1
                vo[("add", bb, sl, ec)] = nv
            for ec in range(DC):
                tok_y[(u, ec)] = b.qX.plan_granule(1)

    g_seq = [(bb, sl, ec, fg) for bb in range(NB) for sl in range(4)
             for ec in range(DC) for fg in range(4)]
    g_pos = {k: i for i, k in enumerate(g_seq)}

    with nc.Block("ffnmain") as blk:
        @blk.sync
        def _(sync):
            for bb in range(NB):
                for sl in range(4):
                    u = bb * 4 + sl
                    if u >= 1:
                        pb, psl = divmod(u - 1, 4)
                        sync.wait_ge(b.vE.h, vo[("add", pb, psl, DC - 1)])
                        sync.wait_ge(*tok_y[(u - 1, DC - 1)])
                    b.qW.begin(sync)
                    for dc in range(DC):
                        b.qW.dma(sync, b.stg[:, dc * 512:dc * 512 + 512],
                                 b.outTb.ap()[dc * 128:(dc + 1) * 128,
                                              bb * S + sl * SL:bb * S + sl * SL + SL])
                    for ec in range(DC):
                        b.qW.dma(sync, b.stg[:, 4096 + ec * 512:4096 + ec * 512 + 512],
                                 b.x3b.ap()[ec * 128:(ec + 1) * 128,
                                            bb * S + sl * SL:bb * S + sl * SL + SL])
                    b.qW.end()
                    for fc in range(32):
                        slot = (fc % 2) * 1024
                        if fc >= 2:
                            sync.wait_ge(b.tM.h, to[("mm1", bb, sl, fc - 1)])
                        b.qW.begin(sync)
                        for hc in range(DC):
                            b.qW.dma(sync,
                                     b.wst[:, slot + hc * 128:slot + hc * 128 + 128],
                                     b.mW1Tb[bb].ap()[hc * 128:(hc + 1) * 128,
                                                      fc * 128:(fc + 1) * 128])
                        b.qW.end()
            b.qW.drain(sync)

        @blk.scalar
        def _(scalar):
            for bb in range(NB):
                for sl in range(4):
                    for fc in range(32):
                        scalar.wait_ge(b.tM.h, to[("mm1", bb, sl, fc)])
                        b.sE.inc(scalar.activation(
                            b.xn[:, fc * 512:fc * 512 + 512],
                            b.psum[4 + (fc % 2)][:, :],
                            AF.Gelu_apprx_tanh,
                            bias=b.sc[:, C_MB1[bb] + fc:C_MB1[bb] + fc + 1],
                            scale=1.0))
                    for ec in range(DC):
                        scalar.wait_ge(b.tM.h, to[("mm2g", bb, sl, ec, 3)])
                        if ec >= 1:
                            scalar.wait_ge(b.vE.h, vo[("add", bb, sl, ec - 1)])
                        b.sE.inc(scalar.activation(
                            b.stg[:, 0:512],
                            b.psum[ec % 4][:, :],
                            AF.Identity,
                            bias=b.sc[:, C_MB2[bb] + ec:C_MB2[bb] + ec + 1],
                            scale=1.0))

        @blk.vector
        def _(vector):
            for bb in range(NB):
                for sl in range(4):
                    for ec in range(DC):
                        xs = b.stg[:, 4096 + ec * 512:4096 + ec * 512 + 512]
                        vector.wait_ge(b.sE.h, so[("cp", bb, sl, ec)])
                        b.vE.inc(vector.tensor_tensor(
                            xs, xs, b.stg[:, 0:512], ALU.add))

        @blk.gpsimd
        def _(gpsimd):
            for bb in range(NB):
                for sl in range(4):
                    u = bb * 4 + sl
                    for ec in range(DC):
                        for fg in range(4):
                            p = g_pos[(bb, sl, ec, fg)]
                            if p >= 2:
                                gpsimd.wait_ge(b.tM.h, to[("mm2g",) + g_seq[p - 2]])
                            b.qX.begin(gpsimd)
                            for fi in range(8):
                                fc = fg * 8 + fi
                                b.qX.dma(gpsimd,
                                         b.wst[:, 2048 + (fg % 2) * 1024 + fi * 128:
                                               2048 + (fg % 2) * 1024 + fi * 128 + 128],
                                         b.mW2Tb[bb].ap()[fc * 128:(fc + 1) * 128,
                                                          ec * 128:(ec + 1) * 128])
                            b.qX.end()
                    for ec in range(DC):
                        xs = b.stg[:, 4096 + ec * 512:4096 + ec * 512 + 512]
                        gpsimd.wait_ge(b.vE.h, vo[("add", bb, sl, ec)])
                        b.qX.begin(gpsimd)
                        b.qX.dma(gpsimd,
                                 b.y.ap()[ec * 128:(ec + 1) * 128,
                                          bb * S + sl * SL:bb * S + sl * SL + SL],
                                 xs)
                        b.qX.end()
            b.qX.drain(gpsimd)

        @blk.tensor
        def _(tensor):
            for bb in range(NB):
                for sl in range(4):
                    u = bb * 4 + sl
                    tensor.wait_ge(*tok_in[u])
                    for fc in range(32):
                        pm = b.psum[4 + (fc % 2)]
                        tensor.wait_ge(*tok_w1[(u, fc)])
                        gfc = u * 32 + fc
                        if gfc >= 2:
                            pu, pf = divmod(gfc - 2, 32)
                            tensor.wait_ge(b.sE.h, so[("gelu", pu // 4, pu % 4, pf)])
                        for hc in range(DC):
                            slot = (fc % 2) * 1024
                            i = tensor.matmul(
                                pm[:, :],
                                b.wst[:, slot + hc * 128:slot + hc * 128 + 128],
                                b.stg[:, hc * 512:hc * 512 + 512],
                                start=(hc == 0), stop=(hc == DC - 1))
                        b.tM.inc(i)
                    tensor.wait_ge(b.sE.h, so[("gelu", bb, sl, 31)])
                    for ec in range(DC):
                        pm = b.psum[ec % 4]
                        keq = u * DC + ec
                        if keq >= 4:
                            kk = keq - 4
                            tensor.wait_ge(b.vE.h,
                                           vo[("add", kk // (4 * DC), (kk // DC) % 4, kk % DC)])
                        for fg in range(4):
                            tensor.wait_ge(*tok_w2[(u, ec, fg)])
                            for fi in range(8):
                                fc = fg * 8 + fi
                                i = tensor.matmul(
                                    pm[:, :],
                                    b.wst[:, 2048 + (fg % 2) * 1024 + fi * 128:
                                          2048 + (fg % 2) * 1024 + fi * 128 + 128],
                                    b.xn[:, fc * 512:fc * 512 + 512],
                                    start=(fc == 0), stop=(fc == 31))
                            b.tM.inc(i)


def _ph_taps(b):
    if not b.taps:
        return
    nc = b.nc
    plans = []
    for t in b.taps:
        src = getattr(b, t)
        rows, cols = src.shape
        for r0 in range(0, rows, 128):
            for c0 in range(0, cols, 4096):
                plans.append((t, r0, c0, min(128, rows - r0), min(4096, cols - c0)))
    toks = [(b.qW.plan_granule(1), b.qW.plan_granule(1)) for _ in plans]
    with nc.Block("taps") as blk:
        @blk.sync
        def _(sync):
            for (t, r0, c0, rr, cc), (ta, tb_) in zip(plans, toks):
                src = getattr(b, t)
                dst = b.taps[t]
                b.qW.begin(sync)
                b.qW.dma(sync, b.stg[0:rr, 0:cc], src.ap()[r0:r0 + rr, c0:c0 + cc])
                b.qW.end()
                sync.wait_ge(*ta)
                b.qW.begin(sync)
                b.qW.dma(sync, dst.ap()[r0:r0 + rr, c0:c0 + cc], b.stg[0:rr, 0:cc])
                b.qW.end()
                sync.wait_ge(*tb_)


def _host_inputs(inputs):
    x = np.asarray(inputs["L0_attn_hidden"], dtype=np.float32)
    tt = lambda a: np.ascontiguousarray(np.asarray(a, dtype=np.float32).T)
    masks = _mask_consts()
    shared = {
        "WqT": tt(inputs["L0_attn_Wq"]),
        "WkT": tt(inputs["L0_attn_Wk"]),
        "WvT": tt(inputs["L0_attn_Wv"]),
        "WoT": tt(inputs["L0_attn_Wo"]),
        "w_gate": np.ascontiguousarray(np.asarray(inputs["L0_ffn_w_gate"], np.float32)),
        "w_up": np.ascontiguousarray(np.asarray(inputs["L0_ffn_w_up"], np.float32)),
        "w_down": np.ascontiguousarray(np.asarray(inputs["L0_ffn_w_down"], np.float32)),
        "wqkvT": tt(inputs["L1_attn_w_qkv"]),
        "woutT": tt(inputs["L1_attn_w_out"]),
        "gamma": np.ascontiguousarray(np.asarray(inputs["L1_attn_gamma"], np.float32).reshape(DC, 128).T),
        "beta": np.ascontiguousarray(np.asarray(inputs["L1_attn_beta"], np.float32).reshape(DC, 128).T),
        "gw1T": tt(inputs["L1_ffn_gate_w1"]),
        "gw2T": tt(inputs["L1_ffn_gate_w2"]),
        "gb1d": np.ascontiguousarray(np.asarray(inputs["L1_ffn_gate_b1"], np.float32).reshape(DC, 128).T),
        "gb2d": np.ascontiguousarray(np.asarray(inputs["L1_ffn_gate_b2"], np.float32).reshape(8, 1)),
        "W1T": tt(inputs["L1_ffn_W1"]),
        "b1d": np.ascontiguousarray(np.asarray(inputs["L1_ffn_b1"], np.float32).reshape(32, 128).T),
        "W2T": tt(inputs["L1_ffn_W2"]),
        "b2d": np.ascontiguousarray(np.asarray(inputs["L1_ffn_b2"], np.float32).reshape(DC, 128).T),
        "TW1T": np.ascontiguousarray(np.asarray(inputs["L1_ffn_TW1"], np.float32).transpose(0, 2, 1)),
        "Tb1": np.ascontiguousarray(np.asarray(inputs["L1_ffn_Tb1"], np.float32)),
        "TW2T": np.ascontiguousarray(np.asarray(inputs["L1_ffn_TW2"], np.float32).transpose(0, 2, 1)),
        "Tb2": np.ascontiguousarray(np.asarray(inputs["L1_ffn_Tb2"], np.float32)),
        "ident_d": np.eye(128, dtype=np.float32),
        "ident2_d": np.ascontiguousarray(np.tile(np.eye(64, dtype=np.float32), (2, 1))),
        "ones_d": np.ones((128, 128), dtype=np.float32),
    }
    for k, v in masks.items():
        shared["mask_" + k] = v
    in_maps = []
    for c in range(8):
        m = dict(shared)
        xs = x[NB * c:NB * (c + 1)]
        m["x0"] = np.ascontiguousarray(xs.transpose(2, 0, 1).reshape(D, T))
        in_maps.append(m)
    return in_maps


def kernel(**inputs):
    in_maps = _host_inputs(inputs)
    nc = build_nc()
    res = run_bass_kernel_spmd(nc, in_maps, core_ids=list(range(8)))
    out = np.empty((16, S, D), dtype=np.float32)
    for c in range(8):
        y = res.results[c]["y"]
        out[NB * c:NB * (c + 1)] = np.asarray(y).reshape(D, NB, S).transpose(1, 2, 0)
    return out
